# version 11
# baseline (speedup 1.0000x reference)
import math

import numpy as np
import ml_dtypes

import concourse.bass as bass
import concourse.mybir as mybir
from concourse.bass_utils import run_bass_kernel_spmd

F32 = mybir.dt.float32
BF16 = mybir.dt.bfloat16
AF = mybir.ActivationFunctionType
ALU = mybir.AluOpType

B, S, H, NH, HD = 4, 2048, 4096, 32, 128
IC = H // 128
NT = S // 128
II = 11008
NJ = II // 128
EPS = 1e-5
SCALE = HD ** -0.5
NEG = -1.0e30
NCORES = 8
NTOK = 1024
NOB = 8
JB = 8
NJB = (NJ + JB - 1) // JB


def _alibi_slopes(n_heads):
    closest_pow2 = 2 ** math.floor(math.log2(n_heads))
    base = 2.0 ** (-(2.0 ** (-(math.log2(closest_pow2) - 3))))
    slopes = [base ** i for i in range(1, closest_pow2 + 1)]
    if closest_pow2 != n_heads:
        extra_base = 2.0 ** (-(2.0 ** (-(math.log2(2 * closest_pow2) - 3))))
        n_extra = min(closest_pow2, n_heads - closest_pow2)
        slopes += [extra_base ** i for i in range(1, 2 * n_extra + 1, 2)]
    return slopes


SLOPES32 = np.asarray(_alibi_slopes(NH), dtype=np.float32)
CUT = 32.0
TH = [min(NT, math.ceil((CUT / float(sl) + 127.0) / 128.0)) for sl in SLOPES32]
_order = sorted(range(NH), key=lambda h: -TH[h])
PAIRS = [(_order[2 * i], _order[2 * i + 1]) for i in range(NH // 2)]
WP = [max(TH[a], TH[b]) for a, b in PAIRS]
UNITS = [(p, t, min(WP[p], t + 1)) for p in range(16) for t in range(16)]
NU = len(UNITS)

CONST_VALS = [EPS, -0.5, SCALE, -1.0, 1.0 / H, -3.0e38, 1.0, 0.0]

_NC_CACHE = [None]


def build_program():
    nc = bass.Bass("TRN2", target_bir_lowering=False)

    xfT = nc.dram_tensor("xfT", [H, S], F32, kind="ExternalInput").ap()
    xoT = nc.dram_tensor("xoT", [H, NTOK], F32, kind="ExternalInput").ap()
    QW = nc.dram_tensor("QW", [NH, 128, IC, 128], F32, kind="ExternalInput").ap()
    KVW = nc.dram_tensor("KVW", [128, IC, 256], F32, kind="ExternalInput").ap()
    OWT = nc.dram_tensor("OWT", [IC, 128, NH, 128], F32, kind="ExternalInput").ap()
    GATE = nc.dram_tensor("GATE", [NJ, 128, IC, 128], F32, kind="ExternalInput").ap()
    UPW = nc.dram_tensor("UPW", [NJ, 128, IC, 128], F32, kind="ExternalInput").ap()
    DWT = nc.dram_tensor("DWT", [NOB, 128, NJ, 512], F32, kind="ExternalInput").ap()
    BPAIR = nc.dram_tensor("BPAIR", [16, 128, S], F32, kind="ExternalInput").ap()
    IDENT = nc.dram_tensor("IDENT", [128, 128], BF16, kind="ExternalInput").ap()
    ONESC = nc.dram_tensor("ONESC", [128, 1], F32, kind="ExternalInput").ap()
    ONESR = nc.dram_tensor("ONESR", [1, 128], F32, kind="ExternalInput").ap()
    CONSTS = nc.dram_tensor("CONSTS", [128, len(CONST_VALS)], F32,
                            kind="ExternalInput").ap()

    xa_sp = nc.dram_tensor("xa_sp", [IC, 128, NTOK], F32, kind="Internal").ap()
    h_sp = nc.dram_tensor("h_sp", [NJ, 128, NTOK], BF16, kind="Internal").ap()
    xo_sp = nc.dram_tensor("xo_sp", [IC, 128, NTOK], F32, kind="Internal").ap()
    attn_sp = nc.dram_tensor("attn_sp", [NH, 128, NTOK], BF16,
                             kind="Internal").ap()
    OUT = nc.dram_tensor("OUT", [IC, 128, NTOK], F32, kind="ExternalOutput").ap()

    xfT_c = xfT.rearrange("(c p) s -> c p s", p=128)
    xoT_c = xoT.rearrange("(c p) n -> c p n", p=128)

    sem = {}
    for name in [
        "cst", "kv", "xf0", "xf1", "xf2", "xf3", "pe0", "ev0",
        "qw0", "qw1", "xo", "pq", "qe", "qph", "asp0", "asp1",
        "arg0", "arg1", "arg2", "arg3",
        "bp0", "bp1", "qk", "ttr2", "exp", "pb", "tp", "te", "pv", "ae",
        "ow0", "ow1", "xr0", "xr1", "wo", "xa", "sq", "acc", "xas0", "xas1",
        "mm2", "var2", "rstd2", "bc2",
        "xara", "xarb", "x2",
        "gu0", "gu1", "ffmm", "silu", "hdve", "hs0", "hs1",
        "dw0", "dw1", "hr0", "hr1", "hr2", "hr3", "hr4", "hr5", "hr6", "hr7",
        "dj", "p5v", "x20", "x21", "x22", "x23",
        "xo50", "xo51", "xo52", "xo53", "x2p", "sq5", "st5",
        "xr60", "xr61", "sq6", "a6", "mm6", "var6", "rstd6", "bc6",
        "xr70", "xr71", "xr72", "xr73", "p7v",
        "outd0", "outd1", "outd2", "outd3",
    ]:
        sem[name] = nc.alloc_semaphore("s_" + name)

    def W(e, s, v):
        if v > 0:
            e.wait_ge(sem[s], v)

    ident = nc.alloc_sbuf_tensor("ident", [128, 128], BF16)
    onesc = nc.alloc_sbuf_tensor("onesc", [128, 1], F32)
    onesr = nc.alloc_sbuf_tensor("onesr", [1, 128], F32)
    Ksb = nc.alloc_sbuf_tensor("Ksb", [128, S], BF16)
    Vsb = nc.alloc_sbuf_tensor("Vsb", [128, S], BF16)
    Bmu = nc.alloc_sbuf_tensor("Bmu", [128, NTOK], F32)
    Brstd = nc.alloc_sbuf_tensor("Brstd", [128, NTOK], F32)
    consts = nc.alloc_sbuf_tensor("consts", [128, len(CONST_VALS)], F32)
    for ci, cv in enumerate(CONST_VALS):
        nc.const_aps.register(F32, cv, consts[:, ci:ci + 1])

    pair_of, off_of = {}, {}
    for p, (h1, h2) in enumerate(PAIRS):
        pair_of[h1], off_of[h1] = p, 0
        pair_of[h2], off_of[h2] = p, 64

    with (
        nc.sbuf_tensor("xfull", [128, IC, S], BF16) as xfull,
        nc.sbuf_tensor("kvw", [128, IC, 256], BF16) as kvw,
        nc.psum_tensor("k_ps", [128, S], F32) as k_ps,
        nc.psum_tensor("v_ps", [128, S], F32) as v_ps,
        nc.Block() as blk,
    ):
        @blk.sync
        def _(e):
            e.dma_start(ident[:], IDENT).then_inc(sem["cst"], 16)
            e.dma_start(onesc[:], ONESC).then_inc(sem["cst"], 16)
            e.dma_start(onesr[:], ONESR).then_inc(sem["cst"], 16)
            e.dma_start(consts[:], CONSTS).then_inc(sem["cst"], 16)
            e.wait_ge(sem["cst"], 64)

        @blk.gpsimd
        def _(e):
            e.dma_start(kvw[:], KVW).then_inc(sem["kv"], 16)
            for ic in range(IC):
                e.dma_start(xfull[:, ic, :],
                            xfT_c[ic]).then_inc(sem["xf%d" % (ic // 8)], 16)

        @blk.tensor
        def _(e):
            W(e, "kv", 16)
            for rnd in range(4):
                W(e, "ev0", rnd)
                for ic in range(IC):
                    if rnd == 0 and ic % 8 == 0:
                        W(e, "xf%d" % (ic // 8), 16 * 8)
                    st, sp = ic == 0, ic == IC - 1
                    if rnd == 0:
                        for seg in range(4):
                            e.matmul(k_ps[:, seg * 512:(seg + 1) * 512],
                                     kvw[:, ic, 0:128],
                                     xfull[:, ic, seg * 512:(seg + 1) * 512],
                                     start=st, stop=sp)
                    for bank in range(4):
                        tt = 4 * rnd + bank
                        mm = e.matmul(v_ps[:, bank * 512:bank * 512 + 128],
                                      xfull[:, ic, tt * 128:(tt + 1) * 128],
                                      kvw[:, ic, 128:256],
                                      start=st, stop=sp)
                mm.then_inc(sem["pe0"], 1)

        @blk.vector
        def _(e):
            for rnd in range(4):
                W(e, "pe0", rnd + 1)
                if rnd == 0:
                    e.tensor_copy(Ksb[:], k_ps[:])
                src = v_ps.rearrange("p (b w) -> p b w", w=512)[:, :, 0:128]
                dst = Vsb[:, rnd * 512:(rnd + 1) * 512].rearrange(
                    "p (b w) -> p b w", w=128)
                e.tensor_copy(dst, src).then_inc(sem["ev0"], 1)

    sc_off, qk_wait = [0] * NU, [0] * NU
    tp_off, tp_wait = [0] * NU, [0] * NU
    lastA = lastB = -1
    for u, (p, t, tw) in enumerate(UNITS):
        if 128 * tw <= 1024:
            if lastA <= lastB:
                sc_off[u], qk_wait[u], lastA = 0, lastA + 1, u
            else:
                sc_off[u], qk_wait[u], lastB = 1024, lastB + 1, u
        else:
            sc_off[u] = 0
            qk_wait[u] = max(lastA, lastB) + 1
            lastA = lastB = u
    lastA = lastB = -1
    for u, (p, t, tw) in enumerate(UNITS):
        if 128 * tw <= 1024:
            if lastA <= lastB:
                tp_off[u], tp_wait[u], lastA = 0, lastA + 1, u
            else:
                tp_off[u], tp_wait[u], lastB = 1024, lastB + 1, u
        else:
            tp_off[u] = 0
            tp_wait[u] = max(lastA, lastB) + 1
            lastA = lastB = u
    NHH = 2 * NH
    QSLOTS = 6
    HSEQ = [h for pr in PAIRS for h in pr]
    q_items = []
    for qh in range(NHH):
        q_items.append(("qhead", qh))
        for ic in range(IC):
            q_items.append(("qmm", qh, ic))

    q_sched = {}
    marker_iter = {}
    q_ptr = [0]
    q_mms = [0]

    def q_take(target, u):
        out = []
        while q_ptr[0] < len(q_items) and q_mms[0] < target:
            it = q_items[q_ptr[0]]
            out.append(it)
            q_ptr[0] += 1
            if it[0] == "qmm":
                q_mms[0] += 1
            else:
                marker_iter[it[1]] = u
        return out

    q_prolog = q_take(256, -1)
    for u in range(NU):
        q_sched[u] = q_take(256 + 8 * (u + 1), u)
    prolog_hh = [qh for qh in range(NHH) if marker_iter.get(qh, 99999) < 0
                 or qh <= 6]
    prolog_hh = sorted(set(qh for qh in prolog_hh if qh <= 6))
    dve_evac_at = {}
    for qh in range(NHH):
        if qh in prolog_hh:
            continue
        if qh + 1 in marker_iter:
            v = max(0, marker_iter[qh + 1])
        else:
            v = min(NU - 1, marker_iter.get(qh, NU - 2) + 2)
        dve_evac_at.setdefault(v, []).append(qh)

    with (
        nc.sbuf_tensor("QP", [128, QSLOTS, NT, 128], BF16) as QP,
        nc.sbuf_tensor("xown", [128, IC, NTOK], BF16) as xown,
        nc.sbuf_tensor("qwb", [128, 2, IC, 128], BF16) as qwb,
        nc.sbuf_tensor("attnP", [128, 2, 2, NTOK], BF16) as attnP,
        nc.sbuf_tensor("bp", [128, 2, S], F32) as bp,
        nc.sbuf_tensor("sbs", [128, 3, S], F32) as sbs,
        nc.sbuf_tensor("pbuf", [128, 3, S], BF16) as pbuf,
        nc.sbuf_tensor("ptr", [128, 2, S], BF16) as ptr,
        nc.sbuf_tensor("negm", [128, 3], F32) as negm,
        nc.sbuf_tensor("rrow", [128, 3], F32) as rrow,
        nc.sbuf_tensor("rinv", [128, 3], F32) as rinv,
        nc.psum_tensor("sc_ps", [128, S], F32) as sc_ps,
        nc.psum_tensor("tp_ps", [128, S], BF16) as tp_ps,
        nc.psum_tensor("at_ps", [128, 128], F32) as at_ps,
        nc.psum_tensor("q_ps", [128, 512], F32) as q_ps,
        nc.Block() as blk,
    ):
        def emit_q(e, items):
            for it in items:
                if it[0] == "qhead":
                    qh = it[1]
                    i = qh // 2
                    if qh % 2 == 0:
                        W(e, "qw%d" % (i % 2), 16 * (i // 2 + 1))
                    W(e, "qe", qh)
                else:
                    _, qh, ic = it
                    i, half = qh // 2, qh % 2
                    mm = e.matmul(q_ps[:, :],
                                  qwb[:, i % 2, ic, :],
                                  xown[:, ic, half * 512:(half + 1) * 512],
                                  start=ic == 0, stop=ic == IC - 1)
                    if ic == IC - 1:
                        mm.then_inc(sem["qph"], 1)

        @blk.gpsimd
        def _(e):
            for ic in range(IC):
                e.dma_start(xown[:, ic, :], xoT_c[ic]).then_inc(sem["xo"], 16)
            for i in range(NH):
                W(e, "qph", 2 * i - 2)
                e.dma_start(qwb[:, i % 2],
                            QW[HSEQ[i]]).then_inc(sem["qw%d" % (i % 2)], 16)

        @blk.sync
        def _(e):
            for p in range(16):
                W(e, "ttr2", 16 * (p - 1))
                e.dma_start(bp[:, p % 2],
                            BPAIR[p]).then_inc(sem["bp%d" % (p % 2)], 16)
                if p >= 1:
                    h1, h2 = PAIRS[p - 1]
                    W(e, "ae", 16 * p)
                    sa = sem["asp%d" % ((p - 1) % 2)]
                    e.dma_start(attn_sp[h1], attnP[:, (p - 1) % 2, 0]).then_inc(
                        sa, 16)
                    e.dma_start(attn_sp[h2], attnP[:, (p - 1) % 2, 1]).then_inc(
                        sa, 16)
            h1, h2 = PAIRS[15]
            W(e, "ae", 16 * 16)
            e.dma_start(attn_sp[h1], attnP[:, 1, 0]).then_inc(sem["asp1"], 16)
            e.dma_start(attn_sp[h2], attnP[:, 1, 1]).then_inc(sem["asp1"], 16)

        @blk.tensor
        def _(e):
            W(e, "xo", 16 * IC)
            emit_q(e, q_prolog)
            for u in range(NU + 3):
                if u < NU:
                    p, t, tw = UNITS[u]
                    w = 128 * tw
                    ks = 128 * (t + 1 - tw)
                    off = sc_off[u]
                    if t == 0:
                        W(e, "qe", 4 * (p + 1))
                    W(e, "ttr2", qk_wait[u])
                    for a in range(0, w, 512):
                        b = min(a + 512, w)
                        mm = e.matmul(sc_ps[:, off + a:off + b],
                                      QP[:, p % QSLOTS, t, :],
                                      Ksb[:, ks + a:ks + b],
                                      start=True, stop=True)
                    mm.then_inc(sem["qk"], 1)
                emit_q(e, q_sched.get(u, []))
                if 0 <= u - 2 < NU:
                    _, _, tw1 = UNITS[u - 2]
                    toff = tp_off[u - 2]
                    W(e, "pb", u - 1)
                    W(e, "te", tp_wait[u - 2])
                    for kt in range(tw1):
                        mm = e.transpose(
                            tp_ps[:, toff + kt * 128:toff + (kt + 1) * 128],
                            pbuf[:, (u - 2) % 3, kt * 128:(kt + 1) * 128],
                            ident[:])
                    mm.then_inc(sem["tp"], 1)
                if 0 <= u - 3 < NU:
                    p1, t1, tw1 = UNITS[u - 3]
                    ks1 = 128 * (t1 + 1 - tw1)
                    W(e, "te", u - 2)
                    W(e, "ae", u - 3)
                    for kt in range(tw1):
                        kk = ks1 + kt * 128
                        mm = e.matmul(
                            at_ps[:, :],
                            Vsb[:, kk:kk + 128],
                            ptr[:, (u - 3) % 2, kt * 128:(kt + 1) * 128],
                            start=kt == 0, stop=kt == tw1 - 1)
                    mm.then_inc(sem["pv"], 1)

        @blk.vector
        def _(e):
            def q_evac(e, qh):
                i, half = qh // 2, qh % 2
                h = HSEQ[i]
                p, off = pair_of[h], off_of[h]
                W(e, "qph", qh + 1)
                W(e, "qk", 16 * (p - QSLOTS + 1))
                src = q_ps[:, :].rearrange("p (t l) -> p t l", l=64)
                e.tensor_scalar(
                    QP[:, p % QSLOTS, 8 * half:8 * (half + 1), off:off + 64],
                    src, SCALE, None, ALU.mult).then_inc(sem["qe"], 1)
                e.drain()

            for hh in prolog_hh:
                q_evac(e, hh)
            for u in range(NU + 1):
                e.drain()
                for qh in dve_evac_at.get(u, []):
                    q_evac(e, qh)
                if u < NU:
                    p, t, tw = UNITS[u]
                    w = 128 * tw
                    off = sc_off[u]
                    W(e, "qk", u + 1)
                    W(e, "exp", u - 2)
                    W(e, "bp%d" % (p % 2), 16 * (p // 2 + 1))
                    e.tensor_tensor(sbs[:, u % 3, :w],
                                    sc_ps[:, off:off + w],
                                    bp[:, p % 2, S - w:], ALU.add)
                    e.drain()
                    e.reduce_max(negm[:, u % 3:u % 3 + 1],
                                 sbs[:, u % 3, :w],
                                 axis=mybir.AxisListType.X,
                                 negate=True).then_inc(sem["ttr2"], 1)
                if 0 <= u - 1 < NU:
                    s1 = (u - 1) % 3
                    W(e, "exp", u)
                    W(e, "tp", u - 3)
                    e.reciprocal(rinv[:, s1:s1 + 1], rrow[:, s1:s1 + 1])
                    e.drain()
                    _, _, tw1 = UNITS[u - 1]
                    w1 = 128 * tw1
                    e.tensor_scalar(pbuf[:, s1, :w1], sbs[:, s1, :w1],
                                    rinv[:, s1:s1 + 1], None,
                                    ALU.mult).then_inc(sem["pb"], 1)

        @blk.scalar
        def _(e):
            for u in range(NU + 4):
                e.drain()
                if 0 <= u - 2 < NU:
                    _, _, tw1 = UNITS[u - 2]
                    w1 = 128 * tw1
                    toff = tp_off[u - 2]
                    W(e, "tp", u - 1)
                    e.copy(ptr[:, (u - 2) % 2, :w1],
                           tp_ps[:, toff:toff + w1]).then_inc(sem["te"], 1)
                if 0 <= u - 4 < NU:
                    p2, t2, _ = UNITS[u - 4]
                    W(e, "pv", u - 3)
                    if t2 == 0:
                        W(e, "asp%d" % (p2 % 2), 32 * (p2 // 2))
                    e.copy(attnP[:, p2 % 2, 0, t2 * 64:(t2 + 1) * 64],
                           at_ps[:, 0:64])
                    e.copy(attnP[:, p2 % 2, 1, t2 * 64:(t2 + 1) * 64],
                           at_ps[:, 64:128]).then_inc(sem["ae"], 1)
                if u < NU:
                    _, _, tw = UNITS[u]
                    w = 128 * tw
                    W(e, "ttr2", u + 1)
                    W(e, "pb", u - 2)
                    e.activation(sbs[:, u % 3, :w], sbs[:, u % 3, :w],
                                 AF.Exp, bias=negm[:, u % 3:u % 3 + 1],
                                 accum_out=rrow[:, u % 3:u % 3 + 1]
                                 ).then_inc(sem["exp"], 1)

    with (
        nc.sbuf_tensor("attnT", [128, NH, NTOK], BF16) as attnT,
        nc.sbuf_tensor("acc", [128, NTOK], F32) as acc,
        nc.sbuf_tensor("accsq", [128, NTOK], F32) as accsq,
        nc.sbuf_tensor("srow0", [1, NTOK], F32) as srow0,
        nc.sbuf_tensor("srow1", [1, NTOK], F32) as srow1,
        nc.sbuf_tensor("srowt", [1, NTOK], F32) as srowt,
        nc.sbuf_tensor("owb", [128, 2, NH, 128], BF16) as owb,
        nc.sbuf_tensor("xres", [128, 2, NTOK], F32) as xres,
        nc.sbuf_tensor("xab", [128, 2, NTOK], F32) as xab,
        nc.sbuf_tensor("sqb", [128, 2, NTOK], F32) as sqb,
        nc.Block() as blk,
    ):
        with (
            nc.psum_tensor("wo_ps", [128, 2 * NTOK], F32) as wo_ps,
            nc.psum_tensor("st_ps", [1, NTOK], F32) as st_ps,
            nc.psum_tensor("st_ps2", [1, NTOK], F32) as st_ps2,
        ):
            @blk.gpsimd
            def _(e):
                for oc in range(IC):
                    W(e, "wo", oc - 1)
                    e.dma_start(owb[:, oc % 2], OWT[oc]).then_inc(
                        sem["ow%d" % (oc % 2)], 16)

            @blk.sync
            def _(e):
                for ocl in range(2):
                    W(e, "asp%d" % ocl, 32 * 8)
                for h in range(NH):
                    e.dma_start(attnT[:, h, :], attn_sp[h]).then_inc(
                        sem["arg%d" % (h // 8)], 16)
                for oc in range(IC):
                    W(e, "xa", oc - 1)
                    e.dma_start(xres[:, oc % 2], xoT_c[oc]).then_inc(
                        sem["xr%d" % (oc % 2)], 16)
                    W(e, "xa", oc + 1)
                    e.dma_start(xa_sp[oc], xab[:, oc % 2]).then_inc(
                        sem["xas%d" % (oc % 2)], 16)

            @blk.tensor
            def _(e):
                for oc in range(IC):
                    W(e, "ow%d" % (oc % 2), 16 * (oc // 2 + 1))
                    W(e, "xa", oc - 1)
                    base = (oc % 2) * NTOK
                    for hh in range(NH):
                        if oc == 0 and hh % 8 == 0:
                            W(e, "arg%d" % (hh // 8), 16 * 8)
                        st, sp = hh == 0, hh == NH - 1
                        for seg in range(2):
                            mm = e.matmul(
                                wo_ps[:, base + seg * 512:
                                      base + (seg + 1) * 512],
                                owb[:, oc % 2, hh, :],
                                attnT[:, hh, seg * 512:(seg + 1) * 512],
                                start=st, stop=sp)
                    mm.then_inc(sem["wo"], 1)
                W(e, "acc", IC)
                for seg in range(2):
                    e.matmul(st_ps[:, seg * 512:(seg + 1) * 512],
                             onesc[:], acc[:, seg * 512:(seg + 1) * 512],
                             start=True, stop=True)
                    mm = e.matmul(st_ps2[:, seg * 512:(seg + 1) * 512],
                                  onesc[:],
                                  accsq[:, seg * 512:(seg + 1) * 512],
                                  start=True, stop=True)
                mm.then_inc(sem["mm2"], 1)

            @blk.scalar
            def _(e):
                for oc in range(IC):
                    e.drain()
                    W(e, "xa", oc + 1)
                    W(e, "acc", oc - 1)
                    e.square(sqb[:, oc % 2],
                             xab[:, oc % 2]).then_inc(sem["sq"], 1)
                W(e, "var2", 1)
                e.activation(srowt[:], srow1[:], AF.Ln, bias=EPS)
                e.drain()
                e.activation(srow1[:], srowt[:], AF.Exp,
                             scale=-0.5).then_inc(sem["rstd2"], 1)

            @blk.vector
            def _(e):
                for oc in range(IC):
                    e.drain()
                    W(e, "wo", oc + 1)
                    W(e, "xr%d" % (oc % 2), 16 * (oc // 2 + 1))
                    W(e, "xas%d" % (oc % 2), 16 * (oc // 2))
                    base = (oc % 2) * NTOK
                    e.tensor_tensor(xab[:, oc % 2],
                                    wo_ps[:, base:base + NTOK],
                                    xres[:, oc % 2],
                                    ALU.add).then_inc(sem["xa"], 1)
                    e.drain()
                    if oc == 0:
                        e.tensor_copy(acc[:], xab[:, 0])
                    else:
                        e.tensor_tensor(acc[:], acc[:], xab[:, oc % 2],
                                        ALU.add)
                    W(e, "sq", oc + 1)
                    if oc == 0:
                        e.tensor_copy(accsq[:], sqb[:, 0]).then_inc(
                            sem["acc"], 1)
                    else:
                        e.tensor_tensor(accsq[:], accsq[:],
                                        sqb[:, oc % 2],
                                        ALU.add).then_inc(sem["acc"], 1)
                W(e, "mm2", 1)
                e.tensor_copy(srow0[:], st_ps[:])
                e.tensor_copy(srow1[:], st_ps2[:])
                e.drain()
                e.tensor_scalar(srow0[:], srow0[:], 1.0 / H, None,
                                ALU.mult)
                e.tensor_scalar(srow1[:], srow1[:], 1.0 / H, None,
                                ALU.mult)
                e.drain()
                e.tensor_tensor(srowt[:], srow0[:], srow0[:], ALU.mult)
                e.drain()
                e.tensor_tensor(srow1[:], srow1[:], srowt[:],
                                ALU.subtract).then_inc(sem["var2"], 1)

        with nc.psum_tensor("bc_ps", [128, 2 * NTOK], F32) as bc_ps:
            @blk.tensor
            def _(e):
                W(e, "rstd2", 1)
                for seg in range(2):
                    e.matmul(bc_ps[:, seg * 512:(seg + 1) * 512],
                             onesr[:], srow0[:, seg * 512:(seg + 1) * 512],
                             start=True, stop=True)
                    mm = e.matmul(
                        bc_ps[:, NTOK + seg * 512:NTOK + (seg + 1) * 512],
                        onesr[:], srow1[:, seg * 512:(seg + 1) * 512],
                        start=True, stop=True)
                mm.then_inc(sem["bc2"], 1)

            @blk.vector
            def _(e):
                W(e, "bc2", 1)
                e.tensor_copy(Bmu[:], bc_ps[:, 0:NTOK])
                e.tensor_copy(Brstd[:], bc_ps[:, NTOK:2 * NTOK])


    with nc.sbuf_tensor("x2bf", [128, IC, NTOK], BF16) as x2bf:
        with (
            nc.sbuf_tensor("xar", [128, 2, NTOK], F32) as xar,
            nc.sbuf_tensor("t3", [128, NTOK], F32) as t3,
            nc.sbuf_tensor("gub", [128, 2, IC, 128], BF16) as gub,
            nc.sbuf_tensor("upb", [128, 2, IC, 128], BF16) as upb,
            nc.sbuf_tensor("sgb", [128, 2, NTOK], F32) as sgb,
            nc.sbuf_tensor("hb", [128, 2, NTOK], BF16) as hb,
            nc.psum_tensor("ff_ps", [128, 4 * NTOK], F32) as ff_ps,
            nc.Block() as blk,
        ):
            @blk.sync
            def _(e):
                W(e, "xas0", 16 * (IC // 2))
                W(e, "xas1", 16 * (IC // 2))
                for oc in range(IC):
                    W(e, "x2", oc - 1)
                    e.dma_start(xar[:, oc % 2], xa_sp[oc]).then_inc(
                        sem["xar" + "ab"[oc % 2]], 16)
                for j in range(NJ):
                    W(e, "hdve", j + 1)
                    e.dma_start(h_sp[j],
                                hb[:, j % 2]).then_inc(sem["hs%d" % (j % 2)], 16)

            @blk.vector
            def _(e):
                for oc in range(IC):
                    e.drain()
                    W(e, "xar" + "ab"[oc % 2], 16 * (oc // 2 + 1))
                    e.tensor_tensor(t3[:], xar[:, oc % 2], Bmu[:], ALU.subtract)
                    e.drain()
                    e.tensor_tensor(x2bf[:, oc, :], t3[:], Brstd[:],
                                    ALU.mult).then_inc(sem["x2"], 1)
                for j in range(NJ):
                    e.drain()
                    W(e, "silu", j + 1)
                    W(e, "hs%d" % (j % 2), 16 * (j // 2))
                    ubase = 2 * NTOK + (j % 2) * NTOK
                    e.tensor_tensor(hb[:, j % 2], sgb[:, j % 2],
                                    ff_ps[:, ubase:ubase + NTOK],
                                    ALU.mult).then_inc(sem["hdve"], 1)

            @blk.gpsimd
            def _(e):
                for j in range(NJ):
                    W(e, "ffmm", j - 1)
                    s = sem["gu%d" % (j % 2)]
                    e.dma_start(gub[:, j % 2], GATE[j]).then_inc(s, 16)
                    e.dma_start(upb[:, j % 2], UPW[j]).then_inc(s, 16)

            @blk.tensor
            def _(e):
                W(e, "x2", IC)
                for j in range(NJ):
                    W(e, "gu%d" % (j % 2), 32 * (j // 2 + 1))
                    W(e, "hdve", j - 1)
                    gbase = (j % 2) * NTOK
                    ubase = 2 * NTOK + (j % 2) * NTOK
                    for ic in range(IC):
                        st, sp = ic == 0, ic == IC - 1
                        for seg in range(2):
                            e.matmul(ff_ps[:, gbase + seg * 512:
                                           gbase + (seg + 1) * 512],
                                     gub[:, j % 2, ic, :],
                                     x2bf[:, ic, seg * 512:(seg + 1) * 512],
                                     start=st, stop=sp)
                            mm = e.matmul(ff_ps[:, ubase + seg * 512:
                                                ubase + (seg + 1) * 512],
                                          upb[:, j % 2, ic, :],
                                          x2bf[:, ic, seg * 512:(seg + 1) * 512],
                                          start=st, stop=sp)
                    mm.then_inc(sem["ffmm"], 1)

            @blk.scalar
            def _(e):
                for j in range(NJ):
                    e.drain()
                    W(e, "ffmm", j + 1)
                    W(e, "hdve", j - 1)
                    gbase = (j % 2) * NTOK
                    e.activation(sgb[:, j % 2], ff_ps[:, gbase:gbase + NTOK],
                                 AF.Silu).then_inc(sem["silu"], 1)

    def jend(jb):
        return min(NJ, (jb + 1) * JB)

    with (
        nc.sbuf_tensor("acc6", [128, NTOK], F32) as acc6,
        nc.sbuf_tensor("asq6", [128, NTOK], F32) as asq6,
        nc.sbuf_tensor("sr60", [1, NTOK], F32) as sr60,
        nc.sbuf_tensor("sr61", [1, NTOK], F32) as sr61,
        nc.sbuf_tensor("sr6t", [1, NTOK], F32) as sr6t,
    ):
        with (
            nc.sbuf_tensor("dwb", [128, 2, JB, 512], BF16) as dwb,
            nc.sbuf_tensor("hin", [128, 8, NTOK], BF16) as hin,
            nc.sbuf_tensor("xar2", [128, 4, NTOK], F32) as xar2,
            nc.sbuf_tensor("x2c", [128, 4, NTOK], F32) as x2c,
            nc.sbuf_tensor("xob", [128, 4, NTOK], F32) as xob,
            nc.sbuf_tensor("sq5b", [128, 4, NTOK], F32) as sq5b,
            nc.sbuf_tensor("t5a", [128, NTOK], F32) as t5a,
            nc.psum_tensor("d_ps", [128, 4 * NTOK], F32) as d_ps,
            nc.Block() as blk,
        ):
            @blk.gpsimd
            def _(e):
                W(e, "hs0", 16 * ((NJ + 1) // 2))
                W(e, "hs1", 16 * (NJ // 2))
                for ob in range(NOB):
                    for jb in range(NJB):
                        if jb >= 2:
                            pob, pjb = ob, jb - 2
                        else:
                            pob, pjb = ob - 1, jb + NJB - 2
                        if pob >= 0:
                            W(e, "dj", pob * NJ + jend(pjb))
                        jl = jend(jb) - jb * JB
                        gb = ob * NJB + jb
                        e.dma_start(dwb[:, gb % 2, 0:jl, :],
                                    DWT[ob][:, jb * JB:jend(jb), :]).then_inc(
                                        sem["dw%d" % (gb % 2)], 16)

            @blk.sync
            def _(e):
                W(e, "hs0", 16 * ((NJ + 1) // 2))
                W(e, "hs1", 16 * (NJ // 2))
                for ob in range(NOB):
                    for j in range(NJ):
                        g = ob * NJ + j
                        W(e, "dj", g - 7)
                        e.dma_start(hin[:, g % 8],
                                    h_sp[j]).then_inc(sem["hr%d" % (g % 8)], 16)

            @blk.scalar
            def _(e):
                W(e, "xas0", 16 * (IC // 2))
                W(e, "xas1", 16 * (IC // 2))
                W(e, "hs0", 16 * ((NJ + 1) // 2))
                W(e, "hs1", 16 * (NJ // 2))

                def spills_squares(e, pb):
                    for ocl in range(4):
                        W(e, "p5v", 4 * pb + ocl + 1)
                        e.dma_start(xo_sp[pb * 4 + ocl], xob[:, ocl]).then_inc(
                            sem["xo5%d" % ocl], 16)
                        W(e, "st5", 4 * (pb - 1) + ocl + 1)
                        e.square(sq5b[:, ocl],
                                 xob[:, ocl]).then_inc(sem["sq5"], 1)

                for ob in range(NOB):
                    e.drain()
                    for ocl in range(4):
                        W(e, "x2p", ob)
                        e.dma_start(xar2[:, ocl],
                                    xa_sp[ob * 4 + ocl]).then_inc(
                                        sem["x2%d" % ocl], 16)
                    if ob >= 1:
                        spills_squares(e, ob - 1)
                e.drain()
                spills_squares(e, NOB - 1)

            @blk.tensor
            def _(e):
                for ob in range(NOB):
                    for j in range(NJ):
                        jb = j // JB
                        gb = ob * NJB + jb
                        g = ob * NJ + j
                        W(e, "dw%d" % (gb % 2), 16 * (gb // 2 + 1))
                        W(e, "hr%d" % (g % 8), 16 * (g // 8 + 1))
                        if j == 0:
                            W(e, "p5v", 4 * ob)
                        for ocl in range(4):
                            for seg in range(2):
                                mm = e.matmul(
                                    d_ps[:, ocl * NTOK + seg * 512:
                                         ocl * NTOK + (seg + 1) * 512],
                                    dwb[:, gb % 2, j - jb * JB,
                                        ocl * 128:(ocl + 1) * 128],
                                    hin[:, g % 8, seg * 512:(seg + 1) * 512],
                                    start=j == 0, stop=j == NJ - 1)
                        mm.then_inc(sem["dj"], 1)

            @blk.vector
            def _(e):
                W(e, "hs0", 16 * ((NJ + 1) // 2))
                W(e, "hs1", 16 * (NJ // 2))

                def stats(e, pb):
                    for ocl in range(4):
                        g = 4 * pb + ocl
                        if g == 0:
                            e.tensor_copy(acc6[:], xob[:, 0])
                        else:
                            e.tensor_tensor(acc6[:], acc6[:], xob[:, ocl],
                                            ALU.add)
                        e.drain()
                        W(e, "sq5", g + 1)
                        if g == 0:
                            e.tensor_copy(asq6[:],
                                          sq5b[:, 0]).then_inc(sem["st5"], 1)
                        else:
                            e.tensor_tensor(asq6[:], asq6[:], sq5b[:, ocl],
                                            ALU.add).then_inc(sem["st5"], 1)
                        e.drain()

                for ob in range(NOB):
                    e.drain()
                    for ocl in range(4):
                        W(e, "x2%d" % ocl, 16 * (ob + 1))
                        e.tensor_tensor(t5a[:], xar2[:, ocl], Bmu[:],
                                        ALU.subtract)
                        e.drain()
                        mm = e.tensor_tensor(x2c[:, ocl], t5a[:], Brstd[:],
                                             ALU.mult)
                        e.drain()
                    mm.then_inc(sem["x2p"], 1)
                    if ob >= 1:
                        stats(e, ob - 1)
                    W(e, "dj", (ob + 1) * NJ)
                    for ocl in range(4):
                        W(e, "xo5%d" % ocl, 16 * ob)
                        W(e, "sq5", 4 * (ob - 1) + ocl + 1)
                        e.tensor_tensor(xob[:, ocl],
                                        d_ps[:, ocl * NTOK:(ocl + 1) * NTOK],
                                        x2c[:, ocl],
                                        ALU.add).then_inc(sem["p5v"], 1)
                        e.drain()
                e.drain()
                stats(e, NOB - 1)

        with nc.Block() as blk:
            with (
                nc.psum_tensor("st6_ps", [1, NTOK], F32) as st6_ps,
                nc.psum_tensor("st6_ps2", [1, NTOK], F32) as st6_ps2,
            ):
                @blk.tensor
                def _(e):
                    W(e, "st5", 4 * NOB)
                    for seg in range(2):
                        e.matmul(st6_ps[:, seg * 512:(seg + 1) * 512], onesc[:],
                                 acc6[:, seg * 512:(seg + 1) * 512],
                                 start=True, stop=True)
                        mm = e.matmul(st6_ps2[:, seg * 512:(seg + 1) * 512],
                                      onesc[:],
                                      asq6[:, seg * 512:(seg + 1) * 512],
                                      start=True, stop=True)
                    mm.then_inc(sem["mm6"], 1)

                @blk.vector
                def _(e):
                    W(e, "mm6", 1)
                    e.tensor_copy(sr60[:], st6_ps[:])
                    e.tensor_copy(sr61[:], st6_ps2[:])
                    e.drain()
                    e.tensor_scalar(sr60[:], sr60[:], 1.0 / H, None, ALU.mult)
                    e.tensor_scalar(sr61[:], sr61[:], 1.0 / H, None, ALU.mult)
                    e.drain()
                    e.tensor_tensor(sr6t[:], sr60[:], sr60[:], ALU.mult)
                    e.drain()
                    e.tensor_tensor(sr61[:], sr61[:], sr6t[:],
                                    ALU.subtract).then_inc(sem["var6"], 1)

                @blk.scalar
                def _(e):
                    W(e, "var6", 1)
                    e.activation(sr6t[:], sr61[:], AF.Ln, bias=EPS)
                    e.drain()
                    e.activation(sr61[:], sr6t[:], AF.Exp,
                                 scale=-0.5).then_inc(sem["rstd6"], 1)

            with nc.psum_tensor("bc6_ps", [128, 2 * NTOK], F32) as bc6_ps:
                @blk.tensor
                def _(e):
                    W(e, "rstd6", 1)
                    for seg in range(2):
                        e.matmul(bc6_ps[:, seg * 512:(seg + 1) * 512], onesr[:],
                                 sr60[:, seg * 512:(seg + 1) * 512],
                                 start=True, stop=True)
                        mm = e.matmul(
                            bc6_ps[:, NTOK + seg * 512:NTOK + (seg + 1) * 512],
                            onesr[:], sr61[:, seg * 512:(seg + 1) * 512],
                            start=True, stop=True)
                    mm.then_inc(sem["bc6"], 1)

                @blk.vector
                def _(e):
                    W(e, "bc6", 1)
                    e.tensor_copy(Bmu[:], bc6_ps[:, 0:NTOK])
                    e.tensor_copy(Brstd[:], bc6_ps[:, NTOK:2 * NTOK])

            with (
                nc.sbuf_tensor("x7", [128, 4, NTOK], F32) as x7,
                nc.sbuf_tensor("t7", [128, NTOK], F32) as t7,
                nc.sbuf_tensor("ob7", [128, 4, NTOK], F32) as ob7,
            ):
                @blk.sync
                def _(e):
                    for ocl in range(4):
                        W(e, "xo5%d" % ocl, 16 * NOB)
                    for oc in range(IC):
                        W(e, "p7v", oc - 3)
                        e.dma_start(x7[:, oc % 4], xo_sp[oc]).then_inc(
                            sem["xr7%d" % (oc % 4)], 16)

                @blk.vector
                def _(e):
                    for oc in range(IC):
                        e.drain()
                        W(e, "xr7%d" % (oc % 4), 16 * (oc // 4 + 1))
                        W(e, "outd%d" % (oc % 4), 16 * (oc // 4))
                        e.tensor_tensor(t7[:], x7[:, oc % 4], Bmu[:],
                                        ALU.subtract)
                        e.drain()
                        e.tensor_tensor(ob7[:, oc % 4], t7[:], Brstd[:],
                                        ALU.mult).then_inc(sem["p7v"], 1)

                @blk.scalar
                def _(e):
                    for oc in range(IC):
                        W(e, "p7v", oc + 1)
                        e.dma_start(OUT[oc], ob7[:, oc % 4]).then_inc(
                            sem["outd%d" % (oc % 4)], 16)
                    for ocl in range(4):
                        e.wait_ge(sem["outd%d" % ocl], 16 * (IC // 4))

    return nc


def _stage_weights(Wq, Wkv, Wo, w_gate_up, w_down):
    QW = np.ascontiguousarray(
        Wq.reshape(IC, 128, NH, 128).transpose(2, 1, 0, 3))
    KVW = np.ascontiguousarray(Wkv.reshape(IC, 128, 256).transpose(1, 0, 2))
    OWT = np.ascontiguousarray(
        Wo.reshape(NH, 128, IC, 128).transpose(2, 1, 0, 3))
    GATE = np.ascontiguousarray(
        w_gate_up[:II].reshape(NJ, 128, IC, 128).transpose(0, 3, 2, 1))
    UPW = np.ascontiguousarray(
        w_gate_up[II:].reshape(NJ, 128, IC, 128).transpose(0, 3, 2, 1))
    DWT = np.ascontiguousarray(
        w_down.reshape(NOB, 512, NJ, 128).transpose(0, 3, 2, 1))
    return QW, KVW, OWT, GATE, UPW, DWT


def _bpair_table(half):
    u = np.arange(S)
    r = np.arange(128)
    rloc = 64 * half + (r % 64)
    rel = (u[None, :] - 1920 - rloc[:, None]).astype(np.int64)
    relf = rel.astype(np.float32)
    out = np.empty((16, 128, S), dtype=np.float32)
    for p, (h1, h2) in enumerate(PAIRS):
        slope = np.where(r < 64, SLOPES32[h1], SLOPES32[h2]).astype(np.float32)
        bias = slope[:, None] * relf
        bias[rel > 0] = np.float32(NEG)
        out[p] = bias
    return out


def _own_idx(half):
    return (np.arange(NT)[:, None] * 128 + 64 * half
            + np.arange(64)[None, :]).ravel()


def kernel(L0_attn_hidden_states, L0_attn_Wq, L0_attn_Wkv, L0_attn_Wo,
           L0_ffn_w_gate_up, L0_ffn_w_down):
    hid = np.asarray(L0_attn_hidden_states, dtype=np.float32)
    Wq = np.asarray(L0_attn_Wq, dtype=np.float32)
    Wkv = np.asarray(L0_attn_Wkv, dtype=np.float32)
    Wo = np.asarray(L0_attn_Wo, dtype=np.float32)
    Wgu = np.asarray(L0_ffn_w_gate_up, dtype=np.float32)
    Wd = np.asarray(L0_ffn_w_down, dtype=np.float32)

    QW, KVW, OWT, GATE, UPW, DWT = _stage_weights(Wq, Wkv, Wo, Wgu, Wd)
    IDENT = np.eye(128, dtype=ml_dtypes.bfloat16)
    ONESC = np.ones((128, 1), dtype=np.float32)
    ONESR = np.ones((1, 128), dtype=np.float32)
    CONSTS = np.ascontiguousarray(
        np.broadcast_to(np.asarray(CONST_VALS, dtype=np.float32),
                        (128, len(CONST_VALS))))
    BP = [_bpair_table(0), _bpair_table(1)]
    IDX = [_own_idx(0), _own_idx(1)]

    in_maps = []
    for c in range(NCORES):
        b, half = c // 2, c % 2
        xfT = np.ascontiguousarray(hid[b].T)
        xoT = np.ascontiguousarray(hid[b].T[:, IDX[half]])
        in_maps.append({
            "xfT": xfT, "xoT": xoT, "QW": QW, "KVW": KVW, "OWT": OWT,
            "GATE": GATE, "UPW": UPW, "DWT": DWT, "BPAIR": BP[half],
            "IDENT": IDENT, "ONESC": ONESC, "ONESR": ONESR, "CONSTS": CONSTS,
        })

    if _NC_CACHE[0] is None:
        _NC_CACHE[0] = build_program()
    nc = _NC_CACHE[0]

    res = run_bass_kernel_spmd(nc, in_maps, core_ids=list(range(NCORES)))

    out = np.empty((B, S, H), dtype=np.float32)
    for c in range(NCORES):
        b, half = c // 2, c % 2
        O = np.asarray(res.results[c]["OUT"])
        arr = O.transpose(2, 0, 1).reshape(NTOK, H)
        out[b, IDX[half], :] = arr
    return out


# version 12
# speedup vs baseline: 1.1625x; 1.1625x over previous
import math

import numpy as np
import ml_dtypes

import concourse.bass as bass
import concourse.mybir as mybir
from concourse.bass_utils import run_bass_kernel_spmd

F32 = mybir.dt.float32
BF16 = mybir.dt.bfloat16
AF = mybir.ActivationFunctionType
ALU = mybir.AluOpType

B, S, H, NH, HD = 4, 2048, 4096, 32, 128
IC = H // 128
NT = S // 128
II = 11008
NJ = II // 128
EPS = 1e-5
SCALE = HD ** -0.5
NEG = -1.0e30
NCORES = 8
NTOK = 1024
NOB = 8
JB = 8
NJB = (NJ + JB - 1) // JB


def _alibi_slopes(n_heads):
    closest_pow2 = 2 ** math.floor(math.log2(n_heads))
    base = 2.0 ** (-(2.0 ** (-(math.log2(closest_pow2) - 3))))
    slopes = [base ** i for i in range(1, closest_pow2 + 1)]
    if closest_pow2 != n_heads:
        extra_base = 2.0 ** (-(2.0 ** (-(math.log2(2 * closest_pow2) - 3))))
        n_extra = min(closest_pow2, n_heads - closest_pow2)
        slopes += [extra_base ** i for i in range(1, 2 * n_extra + 1, 2)]
    return slopes


SLOPES32 = np.asarray(_alibi_slopes(NH), dtype=np.float32)
CUT = 32.0
TH = [min(NT, math.ceil((CUT / float(sl) + 127.0) / 128.0)) for sl in SLOPES32]
_order = sorted(range(NH), key=lambda h: -TH[h])
PAIRS = [(_order[2 * i], _order[2 * i + 1]) for i in range(NH // 2)]
WP = [max(TH[a], TH[b]) for a, b in PAIRS]
UNITS = [(p, t, min(WP[p], t + 1)) for p in range(16) for t in range(16)]
NU = len(UNITS)

CONST_VALS = [EPS, -0.5, SCALE, -1.0, 1.0 / H, -3.0e38, 1.0, 0.0]

_NC_CACHE = [None]


def build_program():
    nc = bass.Bass("TRN2", target_bir_lowering=False)

    xfT = nc.dram_tensor("xfT", [H, S], F32, kind="ExternalInput").ap()
    xoT = nc.dram_tensor("xoT", [H, NTOK], F32, kind="ExternalInput").ap()
    QW = nc.dram_tensor("QW", [NH, 128, IC, 128], F32, kind="ExternalInput").ap()
    KVW = nc.dram_tensor("KVW", [128, IC, 256], F32, kind="ExternalInput").ap()
    OWT = nc.dram_tensor("OWT", [IC, 128, NH, 128], F32, kind="ExternalInput").ap()
    GATE = nc.dram_tensor("GATE", [NJ, 128, IC, 128], F32, kind="ExternalInput").ap()
    UPW = nc.dram_tensor("UPW", [NJ, 128, IC, 128], F32, kind="ExternalInput").ap()
    DWT = nc.dram_tensor("DWT", [NOB, 128, NJ, 512], F32, kind="ExternalInput").ap()
    BPAIR = nc.dram_tensor("BPAIR", [16, 128, S], F32, kind="ExternalInput").ap()
    IDENT = nc.dram_tensor("IDENT", [128, 128], BF16, kind="ExternalInput").ap()
    ONESC = nc.dram_tensor("ONESC", [128, 1], F32, kind="ExternalInput").ap()
    ONESR = nc.dram_tensor("ONESR", [1, 128], F32, kind="ExternalInput").ap()
    CONSTS = nc.dram_tensor("CONSTS", [128, len(CONST_VALS)], F32,
                            kind="ExternalInput").ap()

    xa_sp = nc.dram_tensor("xa_sp", [IC, 128, NTOK], F32, kind="Internal").ap()
    h_sp = nc.dram_tensor("h_sp", [NJ, 128, NTOK], BF16, kind="Internal").ap()
    xo_sp = nc.dram_tensor("xo_sp", [IC, 128, NTOK], F32, kind="Internal").ap()
    attn_sp = nc.dram_tensor("attn_sp", [NH, 128, NTOK], BF16,
                             kind="Internal").ap()
    OUT = nc.dram_tensor("OUT", [IC, 128, NTOK], F32, kind="ExternalOutput").ap()

    xfT_c = xfT.rearrange("(c p) s -> c p s", p=128)
    xoT_c = xoT.rearrange("(c p) n -> c p n", p=128)

    sem = {}
    for name in [
        "cst", "kv", "xf0", "xf1", "xf2", "xf3", "pe0", "ev0",
        "qw0", "qw1", "xo", "pq", "qe", "qph", "asp0", "asp1",
        "arg0", "arg1", "arg2", "arg3",
        "bp0", "bp1", "qk", "ttr2", "exp", "pb", "tp", "te", "pv", "ae",
        "ow0", "ow1", "xr0", "xr1", "wo", "xa", "sq", "acc", "xas0", "xas1",
        "mm2", "var2", "rstd2", "bc2",
        "xara", "xarb", "x2",
        "gu0", "gu1", "ffmm", "silu", "hdve", "hs0", "hs1",
        "dw0", "dw1", "hr0", "hr1", "hr2", "hr3", "hr4", "hr5", "hr6", "hr7",
        "dj", "p5v", "x20", "x21", "x22", "x23",
        "xo50", "xo51", "xo52", "xo53", "x2p", "sq5", "st5",
        "xr60", "xr61", "sq6", "a6", "mm6", "var6", "rstd6", "bc6",
        "xr70", "xr71", "xr72", "xr73", "p7v",
        "outd0", "outd1", "outd2", "outd3",
    ]:
        sem[name] = nc.alloc_semaphore("s_" + name)

    def W(e, s, v):
        if v > 0:
            e.wait_ge(sem[s], v)

    ident = nc.alloc_sbuf_tensor("ident", [128, 128], BF16)
    onesc = nc.alloc_sbuf_tensor("onesc", [128, 1], F32)
    onesr = nc.alloc_sbuf_tensor("onesr", [1, 128], F32)
    Ksb = nc.alloc_sbuf_tensor("Ksb", [128, S], BF16)
    Vsb = nc.alloc_sbuf_tensor("Vsb", [128, S], BF16)
    Bmu = nc.alloc_sbuf_tensor("Bmu", [128, NTOK], F32)
    Brstd = nc.alloc_sbuf_tensor("Brstd", [128, NTOK], F32)
    consts = nc.alloc_sbuf_tensor("consts", [128, len(CONST_VALS)], F32)
    for ci, cv in enumerate(CONST_VALS):
        nc.const_aps.register(F32, cv, consts[:, ci:ci + 1])

    pair_of, off_of = {}, {}
    for p, (h1, h2) in enumerate(PAIRS):
        pair_of[h1], off_of[h1] = p, 0
        pair_of[h2], off_of[h2] = p, 64

    with (
        nc.sbuf_tensor("xfull", [128, IC, S], BF16) as xfull,
        nc.sbuf_tensor("kvw", [128, IC, 256], BF16) as kvw,
        nc.psum_tensor("k_ps", [128, S], F32) as k_ps,
        nc.psum_tensor("v_ps", [128, S], F32) as v_ps,
        nc.Block() as blk,
    ):
        @blk.sync
        def _(e):
            e.dma_start(ident[:], IDENT).then_inc(sem["cst"], 16)
            e.dma_start(onesc[:], ONESC).then_inc(sem["cst"], 16)
            e.dma_start(onesr[:], ONESR).then_inc(sem["cst"], 16)
            e.dma_start(consts[:], CONSTS).then_inc(sem["cst"], 16)
            e.wait_ge(sem["cst"], 64)

        @blk.gpsimd
        def _(e):
            e.dma_start(kvw[:], KVW).then_inc(sem["kv"], 16)
            for ic in range(IC):
                e.dma_start(xfull[:, ic, :],
                            xfT_c[ic]).then_inc(sem["xf%d" % (ic // 8)], 16)

        @blk.tensor
        def _(e):
            W(e, "kv", 16)
            for rnd in range(4):
                W(e, "ev0", rnd)
                for ic in range(IC):
                    if rnd == 0 and ic % 8 == 0:
                        W(e, "xf%d" % (ic // 8), 16 * 8)
                    st, sp = ic == 0, ic == IC - 1
                    if rnd == 0:
                        for seg in range(4):
                            e.matmul(k_ps[:, seg * 512:(seg + 1) * 512],
                                     kvw[:, ic, 0:128],
                                     xfull[:, ic, seg * 512:(seg + 1) * 512],
                                     start=st, stop=sp)
                    for bank in range(4):
                        tt = 4 * rnd + bank
                        mm = e.matmul(v_ps[:, bank * 512:bank * 512 + 128],
                                      xfull[:, ic, tt * 128:(tt + 1) * 128],
                                      kvw[:, ic, 128:256],
                                      start=st, stop=sp)
                mm.then_inc(sem["pe0"], 1)

        @blk.vector
        def _(e):
            for rnd in range(4):
                W(e, "pe0", rnd + 1)
                if rnd == 0:
                    e.tensor_copy(Ksb[:], k_ps[:])
                src = v_ps.rearrange("p (b w) -> p b w", w=512)[:, :, 0:128]
                dst = Vsb[:, rnd * 512:(rnd + 1) * 512].rearrange(
                    "p (b w) -> p b w", w=128)
                e.tensor_copy(dst, src).then_inc(sem["ev0"], 1)

    sc_off, qk_wait = [0] * NU, [0] * NU
    lastA = lastB = -1
    for u, (p, t, tw) in enumerate(UNITS):
        if 128 * tw <= 1024:
            if lastA <= lastB:
                sc_off[u], qk_wait[u], lastA = 0, lastA + 1, u
            else:
                sc_off[u], qk_wait[u], lastB = 1024, lastB + 1, u
        else:
            sc_off[u] = 0
            qk_wait[u] = max(lastA, lastB) + 1
            lastA = lastB = u
    NRND = [1 if 128 * tw <= 1024 else 2 for _, _, tw in UNITS]
    R0 = [0] * NU
    acc_r = 0
    for u in range(NU):
        R0[u] = acc_r
        acc_r += NRND[u]
    NHH = 2 * NH
    QSLOTS = 6
    HSEQ = [h for pr in PAIRS for h in pr]
    q_items = []
    for qh in range(NHH):
        q_items.append(("qhead", qh))
        for ic in range(IC):
            q_items.append(("qmm", qh, ic))

    q_sched = {}
    marker_iter = {}
    q_ptr = [0]
    q_mms = [0]

    def q_take(target, u):
        out = []
        while q_ptr[0] < len(q_items) and q_mms[0] < target:
            it = q_items[q_ptr[0]]
            out.append(it)
            q_ptr[0] += 1
            if it[0] == "qmm":
                q_mms[0] += 1
            else:
                marker_iter[it[1]] = u
        return out

    q_prolog = q_take(256, -1)
    q_sched2 = {}
    for u in range(NU):
        items = q_take(256 + 8 * (u + 1), u)
        if NRND[u] == 2 if u < NU else False:
            pass
        q_sched[u] = items
    for u in range(NU):
        if 0 <= u - 2 < NU and NRND[u - 2] == 2 and q_sched.get(u):
            half = len(q_sched[u]) // 2
            q_sched2[u] = q_sched[u][half:]
            q_sched[u] = q_sched[u][:half]

    prolog_hh = [qh for qh in range(NHH) if marker_iter.get(qh, 99999) < 0
                 or qh <= 6]
    prolog_hh = sorted(set(qh for qh in prolog_hh if qh <= 6))
    dve_evac_at = {}
    for qh in range(NHH):
        if qh in prolog_hh:
            continue
        if qh + 1 in marker_iter:
            v = max(0, marker_iter[qh + 1])
        else:
            v = min(NU - 1, marker_iter.get(qh, NU - 2) + 2)
        dve_evac_at.setdefault(v, []).append(qh)

    with (
        nc.sbuf_tensor("QP", [128, QSLOTS, NT, 128], BF16) as QP,
        nc.sbuf_tensor("xown", [128, IC, NTOK], BF16) as xown,
        nc.sbuf_tensor("qwb", [128, 2, IC, 128], BF16) as qwb,
        nc.sbuf_tensor("attnP", [128, 2, 2, NTOK], BF16) as attnP,
        nc.sbuf_tensor("bp", [128, 2, S], F32) as bp,
        nc.sbuf_tensor("sbs", [128, 3, S], F32) as sbs,
        nc.sbuf_tensor("pbuf", [128, 3, S], BF16) as pbuf,
        nc.sbuf_tensor("ptr", [128, 2, S], BF16) as ptr,
        nc.sbuf_tensor("negm", [128, 3], F32) as negm,
        nc.sbuf_tensor("rrow", [128, 3], F32) as rrow,
        nc.sbuf_tensor("rinv", [128, 3], F32) as rinv,
        nc.psum_tensor("sc_ps", [128, S], F32) as sc_ps,
        nc.psum_tensor("tp_ps", [128, 1024], BF16) as tp_ps,
        nc.psum_tensor("at_ps", [128, 1024], F32) as at_ps,
        nc.psum_tensor("q_ps", [128, 512], F32) as q_ps,
        nc.Block() as blk,
    ):
        def emit_q(e, items):
            for it in items:
                if it[0] == "qhead":
                    qh = it[1]
                    i = qh // 2
                    if qh % 2 == 0:
                        W(e, "qw%d" % (i % 2), 16 * (i // 2 + 1))
                    W(e, "qe", qh)
                else:
                    _, qh, ic = it
                    i, half = qh // 2, qh % 2
                    mm = e.matmul(q_ps[:, :],
                                  qwb[:, i % 2, ic, :],
                                  xown[:, ic, half * 512:(half + 1) * 512],
                                  start=ic == 0, stop=ic == IC - 1)
                    if ic == IC - 1:
                        mm.then_inc(sem["qph"], 1)

        @blk.gpsimd
        def _(e):
            for ic in range(IC):
                e.dma_start(xown[:, ic, :], xoT_c[ic]).then_inc(sem["xo"], 16)
            for i in range(NH):
                W(e, "qph", 2 * i - 2)
                e.dma_start(qwb[:, i % 2],
                            QW[HSEQ[i]]).then_inc(sem["qw%d" % (i % 2)], 16)

        @blk.sync
        def _(e):
            for p in range(16):
                W(e, "ttr2", 16 * (p - 1))
                e.dma_start(bp[:, p % 2],
                            BPAIR[p]).then_inc(sem["bp%d" % (p % 2)], 16)
                if p >= 1:
                    h1, h2 = PAIRS[p - 1]
                    W(e, "ae", 16 * p)
                    sa = sem["asp%d" % ((p - 1) % 2)]
                    e.dma_start(attn_sp[h1], attnP[:, (p - 1) % 2, 0]).then_inc(
                        sa, 16)
                    e.dma_start(attn_sp[h2], attnP[:, (p - 1) % 2, 1]).then_inc(
                        sa, 16)
            h1, h2 = PAIRS[15]
            W(e, "ae", 16 * 16)
            e.dma_start(attn_sp[h1], attnP[:, 1, 0]).then_inc(sem["asp1"], 16)
            e.dma_start(attn_sp[h2], attnP[:, 1, 1]).then_inc(sem["asp1"], 16)

        @blk.tensor
        def _(e):
            W(e, "xo", 16 * IC)
            emit_q(e, q_prolog)
            for u in range(NU + 3):
                if u < NU:
                    p, t, tw = UNITS[u]
                    w = 128 * tw
                    ks = 128 * (t + 1 - tw)
                    off = sc_off[u]
                    if t == 0:
                        W(e, "qe", 4 * (p + 1))
                    W(e, "ttr2", qk_wait[u])
                    for a in range(0, w, 512):
                        b = min(a + 512, w)
                        mm = e.matmul(sc_ps[:, off + a:off + b],
                                      QP[:, p % QSLOTS, t, :],
                                      Ksb[:, ks + a:ks + b],
                                      start=True, stop=True)
                    mm.then_inc(sem["qk"], 1)
                emit_q(e, q_sched.get(u, []))
                if 0 <= u - 2 < NU:
                    _, _, tw1 = UNITS[u - 2]
                    W(e, "pb", u - 1)
                    for rnd in range(NRND[u - 2]):
                        g = R0[u - 2] + rnd
                        k0, k1 = rnd * 8, min(tw1, rnd * 8 + 8)
                        W(e, "te", g)
                        for kt in range(k0, k1):
                            mm = e.transpose(
                                tp_ps[:, (kt - k0) * 128:(kt - k0 + 1) * 128],
                                pbuf[:, (u - 2) % 3, kt * 128:(kt + 1) * 128],
                                ident[:])
                        mm.then_inc(sem["tp"], 1)
                        if rnd == 0 and NRND[u - 2] == 2:
                            emit_q(e, q_sched2.get(u, []))
                if 0 <= u - 3 < NU:
                    p1, t1, tw1 = UNITS[u - 3]
                    ks1 = 128 * (t1 + 1 - tw1)
                    W(e, "te", R0[u - 3] + NRND[u - 3])
                    W(e, "ae", u - 4)
                    slot = ((u - 3) % 2) * 512
                    for kt in range(tw1):
                        kk = ks1 + kt * 128
                        mm = e.matmul(
                            at_ps[:, slot:slot + 128],
                            Vsb[:, kk:kk + 128],
                            ptr[:, (u - 3) % 2, kt * 128:(kt + 1) * 128],
                            start=kt == 0, stop=kt == tw1 - 1)
                    mm.then_inc(sem["pv"], 1)

        @blk.vector
        def _(e):
            def q_evac(e, qh):
                i, half = qh // 2, qh % 2
                h = HSEQ[i]
                p, off = pair_of[h], off_of[h]
                W(e, "qph", qh + 1)
                W(e, "qk", 16 * (p - QSLOTS + 1))
                src = q_ps[:, :].rearrange("p (t l) -> p t l", l=64)
                e.tensor_scalar(
                    QP[:, p % QSLOTS, 8 * half:8 * (half + 1), off:off + 64],
                    src, SCALE, None, ALU.mult).then_inc(sem["qe"], 1)
                e.drain()

            for hh in prolog_hh:
                q_evac(e, hh)
            for u in range(NU + 1):
                e.drain()
                for qh in dve_evac_at.get(u, []):
                    q_evac(e, qh)
                if u < NU:
                    p, t, tw = UNITS[u]
                    w = 128 * tw
                    off = sc_off[u]
                    W(e, "qk", u + 1)
                    W(e, "exp", u - 2)
                    W(e, "bp%d" % (p % 2), 16 * (p // 2 + 1))
                    e.tensor_tensor(sbs[:, u % 3, :w],
                                    sc_ps[:, off:off + w],
                                    bp[:, p % 2, S - w:], ALU.add)
                    e.drain()
                    e.reduce_max(negm[:, u % 3:u % 3 + 1],
                                 sbs[:, u % 3, :w],
                                 axis=mybir.AxisListType.X,
                                 negate=True).then_inc(sem["ttr2"], 1)
                if 0 <= u - 1 < NU:
                    s1 = (u - 1) % 3
                    W(e, "exp", u)
                    W(e, "tp", u - 3)
                    e.reciprocal(rinv[:, s1:s1 + 1], rrow[:, s1:s1 + 1])
                    e.drain()
                    _, _, tw1 = UNITS[u - 1]
                    w1 = 128 * tw1
                    e.tensor_scalar(pbuf[:, s1, :w1], sbs[:, s1, :w1],
                                    rinv[:, s1:s1 + 1], None,
                                    ALU.mult).then_inc(sem["pb"], 1)

        @blk.scalar
        def _(e):
            for u in range(NU + 4):
                e.drain()
                if 0 <= u - 4 < NU:
                    p2, t2, _ = UNITS[u - 4]
                    W(e, "pv", u - 3)
                    if t2 == 0:
                        W(e, "asp%d" % (p2 % 2), 32 * (p2 // 2))
                    slot = ((u - 4) % 2) * 512
                    e.copy(attnP[:, p2 % 2, 0, t2 * 64:(t2 + 1) * 64],
                           at_ps[:, slot:slot + 64])
                    e.copy(attnP[:, p2 % 2, 1, t2 * 64:(t2 + 1) * 64],
                           at_ps[:, slot + 64:slot + 128]).then_inc(sem["ae"], 1)
                if 0 <= u - 2 < NU:
                    _, _, tw1 = UNITS[u - 2]
                    for rnd in range(NRND[u - 2]):
                        g = R0[u - 2] + rnd
                        k0, k1 = rnd * 8, min(tw1, rnd * 8 + 8)
                        W(e, "tp", g + 1)
                        e.copy(ptr[:, (u - 2) % 2, k0 * 128:k1 * 128],
                               tp_ps[:, :(k1 - k0) * 128]).then_inc(sem["te"], 1)
                if u < NU:
                    _, _, tw = UNITS[u]
                    w = 128 * tw
                    W(e, "ttr2", u + 1)
                    W(e, "pb", u - 2)
                    e.activation(sbs[:, u % 3, :w], sbs[:, u % 3, :w],
                                 AF.Exp, bias=negm[:, u % 3:u % 3 + 1],
                                 accum_out=rrow[:, u % 3:u % 3 + 1]
                                 ).then_inc(sem["exp"], 1)

    with (
        nc.sbuf_tensor("attnT", [128, NH, NTOK], BF16) as attnT,
        nc.sbuf_tensor("acc", [128, NTOK], F32) as acc,
        nc.sbuf_tensor("accsq", [128, NTOK], F32) as accsq,
        nc.sbuf_tensor("srow0", [1, NTOK], F32) as srow0,
        nc.sbuf_tensor("srow1", [1, NTOK], F32) as srow1,
        nc.sbuf_tensor("srowt", [1, NTOK], F32) as srowt,
        nc.sbuf_tensor("owb", [128, 2, NH, 128], BF16) as owb,
        nc.sbuf_tensor("xres", [128, 2, NTOK], F32) as xres,
        nc.sbuf_tensor("xab", [128, 2, NTOK], F32) as xab,
        nc.sbuf_tensor("sqb", [128, 2, NTOK], F32) as sqb,
        nc.Block() as blk,
    ):
        with (
            nc.psum_tensor("wo_ps", [128, 2 * NTOK], F32) as wo_ps,
            nc.psum_tensor("st_ps", [1, NTOK], F32) as st_ps,
            nc.psum_tensor("st_ps2", [1, NTOK], F32) as st_ps2,
        ):
            @blk.gpsimd
            def _(e):
                for oc in range(IC):
                    W(e, "wo", oc - 1)
                    e.dma_start(owb[:, oc % 2], OWT[oc]).then_inc(
                        sem["ow%d" % (oc % 2)], 16)

            @blk.sync
            def _(e):
                for ocl in range(2):
                    W(e, "asp%d" % ocl, 32 * 8)
                for h in range(NH):
                    e.dma_start(attnT[:, h, :], attn_sp[h]).then_inc(
                        sem["arg%d" % (h // 8)], 16)
                for oc in range(IC):
                    W(e, "xa", oc - 1)
                    e.dma_start(xres[:, oc % 2], xoT_c[oc]).then_inc(
                        sem["xr%d" % (oc % 2)], 16)
                    W(e, "xa", oc + 1)
                    e.dma_start(xa_sp[oc], xab[:, oc % 2]).then_inc(
                        sem["xas%d" % (oc % 2)], 16)

            @blk.tensor
            def _(e):
                for oc in range(IC):
                    W(e, "ow%d" % (oc % 2), 16 * (oc // 2 + 1))
                    W(e, "xa", oc - 1)
                    base = (oc % 2) * NTOK
                    for hh in range(NH):
                        if oc == 0 and hh % 8 == 0:
                            W(e, "arg%d" % (hh // 8), 16 * 8)
                        st, sp = hh == 0, hh == NH - 1
                        for seg in range(2):
                            mm = e.matmul(
                                wo_ps[:, base + seg * 512:
                                      base + (seg + 1) * 512],
                                owb[:, oc % 2, hh, :],
                                attnT[:, hh, seg * 512:(seg + 1) * 512],
                                start=st, stop=sp)
                    mm.then_inc(sem["wo"], 1)
                W(e, "acc", IC)
                for seg in range(2):
                    e.matmul(st_ps[:, seg * 512:(seg + 1) * 512],
                             onesc[:], acc[:, seg * 512:(seg + 1) * 512],
                             start=True, stop=True)
                    mm = e.matmul(st_ps2[:, seg * 512:(seg + 1) * 512],
                                  onesc[:],
                                  accsq[:, seg * 512:(seg + 1) * 512],
                                  start=True, stop=True)
                mm.then_inc(sem["mm2"], 1)

            @blk.scalar
            def _(e):
                for oc in range(IC):
                    e.drain()
                    W(e, "xa", oc + 1)
                    W(e, "acc", oc - 1)
                    e.square(sqb[:, oc % 2],
                             xab[:, oc % 2]).then_inc(sem["sq"], 1)
                W(e, "var2", 1)
                e.activation(srowt[:], srow1[:], AF.Ln, bias=EPS)
                e.drain()
                e.activation(srow1[:], srowt[:], AF.Exp,
                             scale=-0.5).then_inc(sem["rstd2"], 1)

            @blk.vector
            def _(e):
                for oc in range(IC):
                    e.drain()
                    W(e, "wo", oc + 1)
                    W(e, "xr%d" % (oc % 2), 16 * (oc // 2 + 1))
                    W(e, "xas%d" % (oc % 2), 16 * (oc // 2))
                    base = (oc % 2) * NTOK
                    e.tensor_tensor(xab[:, oc % 2],
                                    wo_ps[:, base:base + NTOK],
                                    xres[:, oc % 2],
                                    ALU.add).then_inc(sem["xa"], 1)
                    e.drain()
                    if oc == 0:
                        e.tensor_copy(acc[:], xab[:, 0])
                    else:
                        e.tensor_tensor(acc[:], acc[:], xab[:, oc % 2],
                                        ALU.add)
                    W(e, "sq", oc + 1)
                    if oc == 0:
                        e.tensor_copy(accsq[:], sqb[:, 0]).then_inc(
                            sem["acc"], 1)
                    else:
                        e.tensor_tensor(accsq[:], accsq[:],
                                        sqb[:, oc % 2],
                                        ALU.add).then_inc(sem["acc"], 1)
                W(e, "mm2", 1)
                e.tensor_copy(srow0[:], st_ps[:])
                e.tensor_copy(srow1[:], st_ps2[:])
                e.drain()
                e.tensor_scalar(srow0[:], srow0[:], 1.0 / H, None,
                                ALU.mult)
                e.tensor_scalar(srow1[:], srow1[:], 1.0 / H, None,
                                ALU.mult)
                e.drain()
                e.tensor_tensor(srowt[:], srow0[:], srow0[:], ALU.mult)
                e.drain()
                e.tensor_tensor(srow1[:], srow1[:], srowt[:],
                                ALU.subtract).then_inc(sem["var2"], 1)

        with nc.psum_tensor("bc_ps", [128, 2 * NTOK], F32) as bc_ps:
            @blk.tensor
            def _(e):
                W(e, "rstd2", 1)
                for seg in range(2):
                    e.matmul(bc_ps[:, seg * 512:(seg + 1) * 512],
                             onesr[:], srow0[:, seg * 512:(seg + 1) * 512],
                             start=True, stop=True)
                    mm = e.matmul(
                        bc_ps[:, NTOK + seg * 512:NTOK + (seg + 1) * 512],
                        onesr[:], srow1[:, seg * 512:(seg + 1) * 512],
                        start=True, stop=True)
                mm.then_inc(sem["bc2"], 1)

            @blk.vector
            def _(e):
                W(e, "bc2", 1)
                e.tensor_copy(Bmu[:], bc_ps[:, 0:NTOK])
                e.tensor_copy(Brstd[:], bc_ps[:, NTOK:2 * NTOK])


    with nc.sbuf_tensor("x2bf", [128, IC, NTOK], BF16) as x2bf:
        with (
            nc.sbuf_tensor("xar", [128, 2, NTOK], F32) as xar,
            nc.sbuf_tensor("t3", [128, NTOK], F32) as t3,
            nc.sbuf_tensor("gub", [128, 2, IC, 128], BF16) as gub,
            nc.sbuf_tensor("upb", [128, 2, IC, 128], BF16) as upb,
            nc.sbuf_tensor("sgb", [128, 2, NTOK], F32) as sgb,
            nc.sbuf_tensor("hb", [128, 2, NTOK], BF16) as hb,
            nc.psum_tensor("ff_ps", [128, 4 * NTOK], F32) as ff_ps,
            nc.Block() as blk,
        ):
            @blk.sync
            def _(e):
                W(e, "xas0", 16 * (IC // 2))
                W(e, "xas1", 16 * (IC // 2))
                for oc in range(IC):
                    W(e, "x2", oc - 1)
                    e.dma_start(xar[:, oc % 2], xa_sp[oc]).then_inc(
                        sem["xar" + "ab"[oc % 2]], 16)
                for j in range(NJ):
                    W(e, "hdve", j + 1)
                    e.dma_start(h_sp[j],
                                hb[:, j % 2]).then_inc(sem["hs%d" % (j % 2)], 16)

            @blk.vector
            def _(e):
                for oc in range(IC):
                    e.drain()
                    W(e, "xar" + "ab"[oc % 2], 16 * (oc // 2 + 1))
                    e.tensor_tensor(t3[:], xar[:, oc % 2], Bmu[:], ALU.subtract)
                    e.drain()
                    e.tensor_tensor(x2bf[:, oc, :], t3[:], Brstd[:],
                                    ALU.mult).then_inc(sem["x2"], 1)
                for j in range(NJ):
                    e.drain()
                    W(e, "silu", j + 1)
                    W(e, "hs%d" % (j % 2), 16 * (j // 2))
                    ubase = 2 * NTOK + (j % 2) * NTOK
                    e.tensor_tensor(hb[:, j % 2], sgb[:, j % 2],
                                    ff_ps[:, ubase:ubase + NTOK],
                                    ALU.mult).then_inc(sem["hdve"], 1)

            @blk.gpsimd
            def _(e):
                for j in range(NJ):
                    W(e, "ffmm", j - 1)
                    s = sem["gu%d" % (j % 2)]
                    e.dma_start(gub[:, j % 2], GATE[j]).then_inc(s, 16)
                    e.dma_start(upb[:, j % 2], UPW[j]).then_inc(s, 16)

            @blk.tensor
            def _(e):
                W(e, "x2", IC)
                for j in range(NJ):
                    W(e, "gu%d" % (j % 2), 32 * (j // 2 + 1))
                    W(e, "hdve", j - 1)
                    gbase = (j % 2) * NTOK
                    ubase = 2 * NTOK + (j % 2) * NTOK
                    for ic in range(IC):
                        st, sp = ic == 0, ic == IC - 1
                        for seg in range(2):
                            e.matmul(ff_ps[:, gbase + seg * 512:
                                           gbase + (seg + 1) * 512],
                                     gub[:, j % 2, ic, :],
                                     x2bf[:, ic, seg * 512:(seg + 1) * 512],
                                     start=st, stop=sp)
                            mm = e.matmul(ff_ps[:, ubase + seg * 512:
                                                ubase + (seg + 1) * 512],
                                          upb[:, j % 2, ic, :],
                                          x2bf[:, ic, seg * 512:(seg + 1) * 512],
                                          start=st, stop=sp)
                    mm.then_inc(sem["ffmm"], 1)

            @blk.scalar
            def _(e):
                for j in range(NJ):
                    e.drain()
                    W(e, "ffmm", j + 1)
                    W(e, "hdve", j - 1)
                    gbase = (j % 2) * NTOK
                    e.activation(sgb[:, j % 2], ff_ps[:, gbase:gbase + NTOK],
                                 AF.Silu).then_inc(sem["silu"], 1)

    def jend(jb):
        return min(NJ, (jb + 1) * JB)

    with (
        nc.sbuf_tensor("acc6", [128, NTOK], F32) as acc6,
        nc.sbuf_tensor("asq6", [128, NTOK], F32) as asq6,
        nc.sbuf_tensor("sr60", [1, NTOK], F32) as sr60,
        nc.sbuf_tensor("sr61", [1, NTOK], F32) as sr61,
        nc.sbuf_tensor("sr6t", [1, NTOK], F32) as sr6t,
    ):
        with (
            nc.sbuf_tensor("dwb", [128, 2, JB, 512], BF16) as dwb,
            nc.sbuf_tensor("hin", [128, 8, NTOK], BF16) as hin,
            nc.sbuf_tensor("xar2", [128, 4, NTOK], F32) as xar2,
            nc.sbuf_tensor("x2c", [128, 4, NTOK], F32) as x2c,
            nc.sbuf_tensor("xob", [128, 4, NTOK], F32) as xob,
            nc.sbuf_tensor("sq5b", [128, 4, NTOK], F32) as sq5b,
            nc.sbuf_tensor("t5a", [128, NTOK], F32) as t5a,
            nc.psum_tensor("d_ps", [128, 4 * NTOK], F32) as d_ps,
            nc.Block() as blk,
        ):
            @blk.gpsimd
            def _(e):
                W(e, "hs0", 16 * ((NJ + 1) // 2))
                W(e, "hs1", 16 * (NJ // 2))
                for ob in range(NOB):
                    for jb in range(NJB):
                        if jb >= 2:
                            pob, pjb = ob, jb - 2
                        else:
                            pob, pjb = ob - 1, jb + NJB - 2
                        if pob >= 0:
                            W(e, "dj", pob * NJ + jend(pjb))
                        jl = jend(jb) - jb * JB
                        gb = ob * NJB + jb
                        e.dma_start(dwb[:, gb % 2, 0:jl, :],
                                    DWT[ob][:, jb * JB:jend(jb), :]).then_inc(
                                        sem["dw%d" % (gb % 2)], 16)

            @blk.sync
            def _(e):
                W(e, "hs0", 16 * ((NJ + 1) // 2))
                W(e, "hs1", 16 * (NJ // 2))
                for ob in range(NOB):
                    for j in range(NJ):
                        g = ob * NJ + j
                        W(e, "dj", g - 7)
                        e.dma_start(hin[:, g % 8],
                                    h_sp[j]).then_inc(sem["hr%d" % (g % 8)], 16)

            @blk.scalar
            def _(e):
                W(e, "xas0", 16 * (IC // 2))
                W(e, "xas1", 16 * (IC // 2))
                W(e, "hs0", 16 * ((NJ + 1) // 2))
                W(e, "hs1", 16 * (NJ // 2))

                def spills_squares(e, pb):
                    for ocl in range(4):
                        W(e, "p5v", 4 * pb + ocl + 1)
                        e.dma_start(xo_sp[pb * 4 + ocl], xob[:, ocl]).then_inc(
                            sem["xo5%d" % ocl], 16)
                        W(e, "st5", 4 * (pb - 1) + ocl + 1)
                        e.square(sq5b[:, ocl],
                                 xob[:, ocl]).then_inc(sem["sq5"], 1)

                for ob in range(NOB):
                    e.drain()
                    for ocl in range(4):
                        W(e, "x2p", ob)
                        e.dma_start(xar2[:, ocl],
                                    xa_sp[ob * 4 + ocl]).then_inc(
                                        sem["x2%d" % ocl], 16)
                    if ob >= 1:
                        spills_squares(e, ob - 1)
                e.drain()
                spills_squares(e, NOB - 1)

            @blk.tensor
            def _(e):
                for ob in range(NOB):
                    for j in range(NJ):
                        jb = j // JB
                        gb = ob * NJB + jb
                        g = ob * NJ + j
                        W(e, "dw%d" % (gb % 2), 16 * (gb // 2 + 1))
                        W(e, "hr%d" % (g % 8), 16 * (g // 8 + 1))
                        if j == 0:
                            W(e, "p5v", 4 * ob)
                        for ocl in range(4):
                            for seg in range(2):
                                mm = e.matmul(
                                    d_ps[:, ocl * NTOK + seg * 512:
                                         ocl * NTOK + (seg + 1) * 512],
                                    dwb[:, gb % 2, j - jb * JB,
                                        ocl * 128:(ocl + 1) * 128],
                                    hin[:, g % 8, seg * 512:(seg + 1) * 512],
                                    start=j == 0, stop=j == NJ - 1)
                        mm.then_inc(sem["dj"], 1)

            @blk.vector
            def _(e):
                W(e, "hs0", 16 * ((NJ + 1) // 2))
                W(e, "hs1", 16 * (NJ // 2))

                def stats(e, pb):
                    for ocl in range(4):
                        g = 4 * pb + ocl
                        if g == 0:
                            e.tensor_copy(acc6[:], xob[:, 0])
                        else:
                            e.tensor_tensor(acc6[:], acc6[:], xob[:, ocl],
                                            ALU.add)
                        e.drain()
                        W(e, "sq5", g + 1)
                        if g == 0:
                            e.tensor_copy(asq6[:],
                                          sq5b[:, 0]).then_inc(sem["st5"], 1)
                        else:
                            e.tensor_tensor(asq6[:], asq6[:], sq5b[:, ocl],
                                            ALU.add).then_inc(sem["st5"], 1)
                        e.drain()

                for ob in range(NOB):
                    e.drain()
                    for ocl in range(4):
                        W(e, "x2%d" % ocl, 16 * (ob + 1))
                        e.tensor_tensor(t5a[:], xar2[:, ocl], Bmu[:],
                                        ALU.subtract)
                        e.drain()
                        mm = e.tensor_tensor(x2c[:, ocl], t5a[:], Brstd[:],
                                             ALU.mult)
                        e.drain()
                    mm.then_inc(sem["x2p"], 1)
                    if ob >= 1:
                        stats(e, ob - 1)
                    W(e, "dj", (ob + 1) * NJ)
                    for ocl in range(4):
                        W(e, "xo5%d" % ocl, 16 * ob)
                        W(e, "sq5", 4 * (ob - 1) + ocl + 1)
                        e.tensor_tensor(xob[:, ocl],
                                        d_ps[:, ocl * NTOK:(ocl + 1) * NTOK],
                                        x2c[:, ocl],
                                        ALU.add).then_inc(sem["p5v"], 1)
                        e.drain()
                e.drain()
                stats(e, NOB - 1)

        with nc.Block() as blk:
            with (
                nc.psum_tensor("st6_ps", [1, NTOK], F32) as st6_ps,
                nc.psum_tensor("st6_ps2", [1, NTOK], F32) as st6_ps2,
            ):
                @blk.tensor
                def _(e):
                    W(e, "st5", 4 * NOB)
                    for seg in range(2):
                        e.matmul(st6_ps[:, seg * 512:(seg + 1) * 512], onesc[:],
                                 acc6[:, seg * 512:(seg + 1) * 512],
                                 start=True, stop=True)
                        mm = e.matmul(st6_ps2[:, seg * 512:(seg + 1) * 512],
                                      onesc[:],
                                      asq6[:, seg * 512:(seg + 1) * 512],
                                      start=True, stop=True)
                    mm.then_inc(sem["mm6"], 1)

                @blk.vector
                def _(e):
                    W(e, "mm6", 1)
                    e.tensor_copy(sr60[:], st6_ps[:])
                    e.tensor_copy(sr61[:], st6_ps2[:])
                    e.drain()
                    e.tensor_scalar(sr60[:], sr60[:], 1.0 / H, None, ALU.mult)
                    e.tensor_scalar(sr61[:], sr61[:], 1.0 / H, None, ALU.mult)
                    e.drain()
                    e.tensor_tensor(sr6t[:], sr60[:], sr60[:], ALU.mult)
                    e.drain()
                    e.tensor_tensor(sr61[:], sr61[:], sr6t[:],
                                    ALU.subtract).then_inc(sem["var6"], 1)

                @blk.scalar
                def _(e):
                    W(e, "var6", 1)
                    e.activation(sr6t[:], sr61[:], AF.Ln, bias=EPS)
                    e.drain()
                    e.activation(sr61[:], sr6t[:], AF.Exp,
                                 scale=-0.5).then_inc(sem["rstd6"], 1)

            with nc.psum_tensor("bc6_ps", [128, 2 * NTOK], F32) as bc6_ps:
                @blk.tensor
                def _(e):
                    W(e, "rstd6", 1)
                    for seg in range(2):
                        e.matmul(bc6_ps[:, seg * 512:(seg + 1) * 512], onesr[:],
                                 sr60[:, seg * 512:(seg + 1) * 512],
                                 start=True, stop=True)
                        mm = e.matmul(
                            bc6_ps[:, NTOK + seg * 512:NTOK + (seg + 1) * 512],
                            onesr[:], sr61[:, seg * 512:(seg + 1) * 512],
                            start=True, stop=True)
                    mm.then_inc(sem["bc6"], 1)

                @blk.vector
                def _(e):
                    W(e, "bc6", 1)
                    e.tensor_copy(Bmu[:], bc6_ps[:, 0:NTOK])
                    e.tensor_copy(Brstd[:], bc6_ps[:, NTOK:2 * NTOK])

            with (
                nc.sbuf_tensor("x7", [128, 4, NTOK], F32) as x7,
                nc.sbuf_tensor("t7", [128, NTOK], F32) as t7,
                nc.sbuf_tensor("ob7", [128, 4, NTOK], F32) as ob7,
            ):
                @blk.sync
                def _(e):
                    for ocl in range(4):
                        W(e, "xo5%d" % ocl, 16 * NOB)
                    for oc in range(IC):
                        W(e, "p7v", oc - 3)
                        e.dma_start(x7[:, oc % 4], xo_sp[oc]).then_inc(
                            sem["xr7%d" % (oc % 4)], 16)

                @blk.vector
                def _(e):
                    for oc in range(IC):
                        e.drain()
                        W(e, "xr7%d" % (oc % 4), 16 * (oc // 4 + 1))
                        W(e, "outd%d" % (oc % 4), 16 * (oc // 4))
                        e.tensor_tensor(t7[:], x7[:, oc % 4], Bmu[:],
                                        ALU.subtract)
                        e.drain()
                        e.tensor_tensor(ob7[:, oc % 4], t7[:], Brstd[:],
                                        ALU.mult).then_inc(sem["p7v"], 1)

                @blk.scalar
                def _(e):
                    for oc in range(IC):
                        W(e, "p7v", oc + 1)
                        e.dma_start(OUT[oc], ob7[:, oc % 4]).then_inc(
                            sem["outd%d" % (oc % 4)], 16)
                    for ocl in range(4):
                        e.wait_ge(sem["outd%d" % ocl], 16 * (IC // 4))

    return nc


def _stage_weights(Wq, Wkv, Wo, w_gate_up, w_down):
    QW = np.ascontiguousarray(
        Wq.reshape(IC, 128, NH, 128).transpose(2, 1, 0, 3))
    KVW = np.ascontiguousarray(Wkv.reshape(IC, 128, 256).transpose(1, 0, 2))
    OWT = np.ascontiguousarray(
        Wo.reshape(NH, 128, IC, 128).transpose(2, 1, 0, 3))
    GATE = np.ascontiguousarray(
        w_gate_up[:II].reshape(NJ, 128, IC, 128).transpose(0, 3, 2, 1))
    UPW = np.ascontiguousarray(
        w_gate_up[II:].reshape(NJ, 128, IC, 128).transpose(0, 3, 2, 1))
    DWT = np.ascontiguousarray(
        w_down.reshape(NOB, 512, NJ, 128).transpose(0, 3, 2, 1))
    return QW, KVW, OWT, GATE, UPW, DWT


def _bpair_table(half):
    u = np.arange(S)
    r = np.arange(128)
    rloc = 64 * half + (r % 64)
    rel = (u[None, :] - 1920 - rloc[:, None]).astype(np.int64)
    relf = rel.astype(np.float32)
    out = np.empty((16, 128, S), dtype=np.float32)
    for p, (h1, h2) in enumerate(PAIRS):
        slope = np.where(r < 64, SLOPES32[h1], SLOPES32[h2]).astype(np.float32)
        bias = slope[:, None] * relf
        bias[rel > 0] = np.float32(NEG)
        out[p] = bias
    return out


def _own_idx(half):
    return (np.arange(NT)[:, None] * 128 + 64 * half
            + np.arange(64)[None, :]).ravel()


def kernel(L0_attn_hidden_states, L0_attn_Wq, L0_attn_Wkv, L0_attn_Wo,
           L0_ffn_w_gate_up, L0_ffn_w_down):
    hid = np.asarray(L0_attn_hidden_states, dtype=np.float32)
    Wq = np.asarray(L0_attn_Wq, dtype=np.float32)
    Wkv = np.asarray(L0_attn_Wkv, dtype=np.float32)
    Wo = np.asarray(L0_attn_Wo, dtype=np.float32)
    Wgu = np.asarray(L0_ffn_w_gate_up, dtype=np.float32)
    Wd = np.asarray(L0_ffn_w_down, dtype=np.float32)

    QW, KVW, OWT, GATE, UPW, DWT = _stage_weights(Wq, Wkv, Wo, Wgu, Wd)
    IDENT = np.eye(128, dtype=ml_dtypes.bfloat16)
    ONESC = np.ones((128, 1), dtype=np.float32)
    ONESR = np.ones((1, 128), dtype=np.float32)
    CONSTS = np.ascontiguousarray(
        np.broadcast_to(np.asarray(CONST_VALS, dtype=np.float32),
                        (128, len(CONST_VALS))))
    BP = [_bpair_table(0), _bpair_table(1)]
    IDX = [_own_idx(0), _own_idx(1)]

    in_maps = []
    for c in range(NCORES):
        b, half = c // 2, c % 2
        xfT = np.ascontiguousarray(hid[b].T)
        xoT = np.ascontiguousarray(hid[b].T[:, IDX[half]])
        in_maps.append({
            "xfT": xfT, "xoT": xoT, "QW": QW, "KVW": KVW, "OWT": OWT,
            "GATE": GATE, "UPW": UPW, "DWT": DWT, "BPAIR": BP[half],
            "IDENT": IDENT, "ONESC": ONESC, "ONESR": ONESR, "CONSTS": CONSTS,
        })

    if _NC_CACHE[0] is None:
        _NC_CACHE[0] = build_program()
    nc = _NC_CACHE[0]

    res = run_bass_kernel_spmd(nc, in_maps, core_ids=list(range(NCORES)))

    out = np.empty((B, S, H), dtype=np.float32)
    for c in range(NCORES):
        b, half = c // 2, c % 2
        O = np.asarray(res.results[c]["OUT"])
        arr = O.transpose(2, 0, 1).reshape(NTOK, H)
        out[b, IDX[half], :] = arr
    return out
